# version 01
# baseline (speedup 1.0000x reference)
import numpy as np
import ml_dtypes
import concourse.bass as bass
import concourse.mybir as mybir
from concourse.bass_utils import run_bass_kernel_spmd

F32 = mybir.dt.float32
BF16 = mybir.dt.bfloat16
I8 = mybir.dt.int8
I32 = mybir.dt.int32
AF = mybir.ActivationFunctionType
ALU = mybir.AluOpType
AX = mybir.AxisListType

L = 2048
DM = 1024
E = 2048
N = 16
R = 64
KC = DM // 128
GC = E // 128
TT = L // 128
EPS = 1e-5
NOCT = 8

class _PhaseStop(Exception):
    pass


AB_COLS = 63488
AF_COLS = 18432


def build_program(debug=False, phase_limit=99):
    nc = bass.Bass("TRN2", target_bir_lowering=False, debug=False)

    kind_scr = "ExternalOutput" if debug else "Internal"

    d_u = nc.dram_tensor("u", [L, DM], F32, kind="ExternalInput").ap()
    d_w_inT = nc.dram_tensor("w_inT", [DM, 2 * E], F32, kind="ExternalInput").ap()
    d_w_xT = nc.dram_tensor("w_xT", [128, GC * 96], F32, kind="ExternalInput").ap()
    d_w_dtT = nc.dram_tensor("w_dtT", [R, E], F32, kind="ExternalInput").ap()
    d_w_outT = nc.dram_tensor("w_outT", [E, DM], F32, kind="ExternalInput").ap()
    d_convd = nc.dram_tensor("convd", [128, 4 * GC * 128], F32, kind="ExternalInput").ap()
    d_ffn_wT = nc.dram_tensor("ffn_wT", [DM, DM], I8, kind="ExternalInput").ap()
    d_wnorm = nc.dram_tensor("wnorm_c", [128, KC], F32, kind="ExternalInput").ap()
    d_convb = nc.dram_tensor("convb_c", [128, GC], F32, kind="ExternalInput").ap()
    d_bdt = nc.dram_tensor("bdt_c", [128, GC], F32, kind="ExternalInput").ap()
    d_Dpar = nc.dram_tensor("dpar_c", [128, GC], F32, kind="ExternalInput").ap()
    d_Alog = nc.dram_tensor("alog_c", [128, GC * N], F32, kind="ExternalInput").ap()
    d_fscale = nc.dram_tensor("fscale_c", [128, KC], F32, kind="ExternalInput").ap()
    d_fbias = nc.dram_tensor("fbias_c", [128, KC], F32, kind="ExternalInput").ap()
    d_identf = nc.dram_tensor("identf", [128, 128], F32, kind="ExternalInput").ap()
    d_sel = nc.dram_tensor("sel32", [32, 32 * 128], BF16, kind="ExternalInput").ap()
    d_identb = nc.dram_tensor("identb_in", [128, 128], BF16, kind="ExternalInput").ap()
    d_zero4 = nc.dram_tensor("zero4", [128, 4], BF16, kind="ExternalInput").ap()

    d_out = nc.dram_tensor("out", [L, DM], F32, kind="ExternalOutput").ap()

    s_xn = nc.dram_tensor("s_xn", [L, DM], F32, kind=kind_scr).ap()
    s_un = nc.dram_tensor("s_un", [KC, 128, L], BF16, kind=kind_scr).ap()
    s_xr = nc.dram_tensor("s_xr", [GC, 128, L], BF16, kind=kind_scr).ap()
    s_xc = nc.dram_tensor("s_xc", [GC, 128, L], BF16, kind=kind_scr).ap()
    s_z = nc.dram_tensor("s_z", [GC, 128, L], BF16, kind=kind_scr).ap()
    s_dt = nc.dram_tensor("s_dt", [GC, 128, L], F32, kind=kind_scr).ap()
    s_y0 = nc.dram_tensor("s_y0", [GC, 128, L], F32, kind="Internal").ap()
    s_y2 = nc.dram_tensor("s_y2", [GC, 128, L], BF16, kind=kind_scr).ap()
    s_qf = nc.dram_tensor("s_qf", [KC, 128, L], BF16, kind=kind_scr).ap()
    s_dbg = nc.dram_tensor("s_dbg", [128, 112], F32, kind=kind_scr).ap()
    s_dbg2 = nc.dram_tensor("s_dbg2", [128, 2 * L], BF16, kind=kind_scr).ap()
    s_dbg3 = nc.dram_tensor("s_dbg3", [128, L], F32, kind=kind_scr).ap()
    cc_in = nc.dram_tensor("cc_in", [1, 16], F32, kind="Internal").ap()
    cc_out = nc.dram_tensor("cc_out", [1, 16], F32, kind="Internal", addr_space="Shared").ap()

    streams = {"sync": [], "scalar": [], "vector": [], "tensor": [], "gpsimd": []}

    def emit(eng, fn):
        streams[eng].append(fn)

    class Ctr:
        def __init__(self, sem):
            self.sem = sem
            self.val = 0

    import contextlib
    with contextlib.ExitStack() as _stack:
        _e = _stack.enter_context
        v_wnorm = _e(nc.sbuf_tensor("v_wnorm", [128, KC], F32))
        v_convb = _e(nc.sbuf_tensor("v_convb", [128, GC], F32))
        v_bdt = _e(nc.sbuf_tensor("v_bdt", [128, GC], F32))
        v_dpar = _e(nc.sbuf_tensor("v_dpar", [128, GC], F32))
        v_negA = _e(nc.sbuf_tensor("v_negA", [128, GC * N], F32))
        v_fsc = _e(nc.sbuf_tensor("v_fsc", [128, KC], F32))
        v_fbias = _e(nc.sbuf_tensor("v_fbias", [128, KC], F32))
        identf_ = _e(nc.sbuf_tensor("sb_identf", [128, 128], F32))
        identb_ = _e(nc.sbuf_tensor("identb", [128, 128], BF16))
        onesb_ = _e(nc.sbuf_tensor("onesb", [1, 128], BF16))
        onesf_ = _e(nc.sbuf_tensor("onesf", [1, 128], F32))
        stats_ = _e(nc.sbuf_tensor("stats", [128, 64], F32))
        mcol_ = _e(nc.sbuf_tensor("mcol", [128, TT], F32))
        iscol_ = _e(nc.sbuf_tensor("iscol", [128, TT], F32))
        irms_ = _e(nc.sbuf_tensor("irms", [128, TT], F32))
        smal_ = _e(nc.sbuf_tensor("smal", [128, 16], F32))
        qscale_ = _e(nc.sbuf_tensor("qscale", [128, 1], F32))
        fsc2_ = _e(nc.sbuf_tensor("fsc2", [128, KC], F32))
        cc_sb_ = _e(nc.sbuf_tensor("cc_sb", [1, 16], F32))
        sel32_ = _e(nc.sbuf_tensor("sb_sel32", [32, 32 * 128], BF16))
        ARF = _e(nc.sbuf_tensor("ARF", [128, AF_COLS], F32))
        ARB = _e(nc.sbuf_tensor("ARB", [128, AB_COLS], BF16))
        pb0 = _e(nc.psum_tensor("pb0", [128, 512], F32))
        pb1 = _e(nc.psum_tensor("pb1", [128, 512], F32))
        pb2 = _e(nc.psum_tensor("pb2", [128, 512], F32))
        pb3 = _e(nc.psum_tensor("pb3", [128, 512], F32))
        pb4 = _e(nc.psum_tensor("pb4", [128, 512], F32))
        pb5 = _e(nc.psum_tensor("pb5", [128, 512], F32))
        pb6 = _e(nc.psum_tensor("pb6", [128, 512], F32))
        pb7 = _e(nc.psum_tensor("pb7", [128, 512], F32))
        sm_dma = _e(nc.semaphore("sm_dma"))
        sm_gdma = _e(nc.semaphore("sm_gdma"))
        sm_t = _e(nc.semaphore("sm_t"))
        sm_v = _e(nc.semaphore("sm_v"))
        sm_s = _e(nc.semaphore("sm_s"))
        sm_g = _e(nc.semaphore("sm_g"))
        block = _e(nc.Block())
        identf = identf_[:]
        identb = identb_[:]
        onesb = onesb_[:]
        onesf = onesf_[:]
        stats = stats_
        mcol, iscol, irms = mcol_, iscol_, irms_
        smal, qscale, fsc2, cc_sb = smal_, qscale_, fsc2_, cc_sb_
        i32scr = ARF[:, 16384:17408].bitcast(I32)
        stg8 = ARF[:, 17920:18432].bitcast(I8)[:, 0:DM]
        sel32 = sel32_
        pbank = [pb0, pb1, pb2, pb3, pb4, pb5, pb6, pb7]

        c_dma, c_gdma = Ctr(sm_dma), Ctr(sm_gdma)
        c_t, c_v, c_s, c_g = Ctr(sm_t), Ctr(sm_v), Ctr(sm_s), Ctr(sm_g)

        def dma_in(dst, src):
            c_dma.val += 16
            emit("sync", lambda q, d=dst, s=src: q.dma_start(out=d, in_=s).then_inc(sm_dma, 16))
            return c_dma.val

        def dma_g(dst, src):
            c_gdma.val += 16
            emit("gpsimd", lambda q, d=dst, s=src: q.dma_start(out=d, in_=s).then_inc(sm_gdma, 16))
            return c_gdma.val

        def w(eng, ctr, tgt=None):
            t = ctr.val if tgt is None else tgt
            if t > 0:
                emit(eng, lambda q, c=ctr, tv=t: q.wait_ge(c.sem, tv))

        def ten(fn):
            c_t.val += 1
            emit("tensor", lambda q, f=fn: f(q).then_inc(sm_t, 1))
            return c_t.val

        def ten0(fn):
            emit("tensor", lambda q, f=fn: f(q))

        def ten_mark():
            c_t.val += 1
            emit("tensor", lambda q: q.nop().then_inc(sm_t, 1))
            return c_t.val

        def vec(fn):
            c_v.val += 1
            emit("vector", lambda q, f=fn: f(q).then_inc(sm_v, 1))
            return c_v.val

        def vsync():
            emit("vector", lambda q, tv=c_v.val: q.wait_ge(sm_v, tv))

        def sca(fn):
            c_s.val += 1
            emit("scalar", lambda q, f=fn: f(q).then_inc(sm_s, 1))
            return c_s.val

        def gps(fn):
            c_g.val += 1
            emit("gpsimd", lambda q, f=fn: f(q).then_inc(sm_g, 1))
            return c_g.val

        try:
            for dst, src in ((v_wnorm[:], d_wnorm), (v_convb[:], d_convb), (v_bdt[:], d_bdt),
                             (v_dpar[:], d_Dpar), (v_negA[:], d_Alog), (v_fsc[:], d_fscale),
                             (v_fbias[:], d_fbias), (identf, d_identf), (sel32[:], d_sel),
                             (identb, d_identb)):
                dma_in(dst, src)
            w("vector", c_dma)
            vec(lambda q: q.memset(onesb, 1.0))
            vec(lambda q: q.memset(onesf, 1.0))
            w("scalar", c_dma)
            sca(lambda q: q.activation(out=v_negA[:], in_=v_negA[:], func=AF.Exp))
            w("vector", c_s)
            vec(lambda q: q.tensor_scalar(v_negA[:], v_negA[:], -1.0, None, ALU.mult))

            u4 = ARF[:, 0:4096]
            xn_t = ARF[:, 4096:5120]
            un_t = ARF[:, 5120:6144]
            scr = ARF[:, 6144:7168]
            for b in range(4):
                loads = []
                for i in range(4):
                    tt = b * 4 + i
                    w("sync", c_s)
                    w("sync", c_v)
                    loads.append(dma_in(u4[:, i * DM:(i + 1) * DM], d_u[tt * 128:(tt + 1) * 128, :]))
                for i in range(4):
                    tt = b * 4 + i
                    ut = u4[:, i * DM:(i + 1) * DM]
                    w("vector", c_dma, loads[i])
                    w("vector", c_s)
                    vec(lambda q, ut=ut, tt=tt: q.tensor_scalar(
                        scr, ut, 1.0, 0.0, ALU.mult, ALU.add, accum_out=stats[:, tt:tt + 1]))
                    w("scalar", c_dma, loads[i])
                    w("scalar", c_v)
                    sca(lambda q, ut=ut, tt=tt: q.activation(
                        out=scr, in_=ut, func=AF.Square, accum_out=stats[:, 16 + tt:16 + tt + 1]))
            def rsqrt_nr(col, y, t, n=6):
                vec(lambda q, y=y: q.memset(y, 1.0))
                vsync()
                for _ in range(n):
                    vec(lambda q, col=col, y=y, t=t: q.tensor_tensor(t, col, y, ALU.mult))
                    vsync()
                    vec(lambda q, y=y, t=t: q.tensor_tensor(t, t, y, ALU.mult))
                    vsync()
                    vec(lambda q, t=t: q.tensor_scalar(t, t, -0.5, 1.5, ALU.mult, ALU.add))
                    vsync()
                    vec(lambda q, y=y, t=t: q.tensor_tensor(y, y, t, ALU.mult))
                    vsync()
                vec(lambda q, col=col, y=y: q.tensor_copy(col, y))
                vsync()

            w("vector", c_s)
            vsync()
            vec(lambda q: q.tensor_scalar(mcol[:], stats[:, 0:16], 1.0 / DM, None, ALU.mult))
            vec(lambda q: q.tensor_scalar(iscol[:], stats[:, 16:32], 1.0 / DM, None, ALU.mult))
            vsync()
            vec(lambda q: q.tensor_tensor(smal[:, 0:16], mcol[:], mcol[:], ALU.mult))
            vsync()
            vec(lambda q: q.tensor_tensor(iscol[:], iscol[:], smal[:, 0:16], ALU.subtract))
            vsync()
            vec(lambda q: q.tensor_scalar(iscol[:], iscol[:], EPS, None, ALU.add))
            vsync()
            rsqrt_nr(iscol[:], smal[:, 0:16], irms[:])

            for b in range(4):
                loads = []
                for i in range(4):
                    tt = b * 4 + i
                    w("sync", c_s)
                    w("sync", c_v)
                    loads.append(dma_in(u4[:, i * DM:(i + 1) * DM], d_u[tt * 128:(tt + 1) * 128, :]))
                for i in range(4):
                    tt = b * 4 + i
                    ut = u4[:, i * DM:(i + 1) * DM]
                    w("vector", c_dma, loads[i])
                    w("vector", c_gdma)
                    w("vector", c_s)
                    vec(lambda q, ut=ut, tt=tt: q.tensor_scalar(
                        xn_t, ut, mcol[:, tt:tt + 1], iscol[:, tt:tt + 1], ALU.subtract, ALU.mult))
                    vec(lambda q, tt=tt: q.tensor_reduce(
                        stats[:, 32 + tt:32 + tt + 1], xn_t, AX.X, ALU.max, apply_absolute_value=True))
                    w("scalar", c_v)
                    sca(lambda q, tt=tt: q.activation(
                        out=scr, in_=xn_t, func=AF.Square, accum_out=stats[:, 48 + tt:48 + tt + 1]))
                    w("gpsimd", c_v)
                    dma_g(s_xn[tt * 128:(tt + 1) * 128, :], xn_t)

            w("vector", c_s)
            vsync()
            vec(lambda q: q.tensor_scalar(irms[:], stats[:, 48:64], 1.0 / DM, None, ALU.mult))
            vsync()
            vec(lambda q: q.tensor_scalar(irms[:], irms[:], EPS, None, ALU.add))
            vsync()
            rsqrt_nr(irms[:], smal[:, 0:16], mcol[:])
            t_irms = c_v.val

            if debug:
                w("gpsimd", c_v)
                dma_g(s_dbg[:, 0:64], stats[:])
                dma_g(s_dbg[:, 64:80], mcol[:])
                dma_g(s_dbg[:, 80:96], iscol[:])
                dma_g(s_dbg[:, 96:112], irms[:])
            vsync()
            vec(lambda q: q.tensor_reduce(smal[:, 0:1], stats[:, 32:48], AX.X, ALU.max))
            vsync()
            w("tensor", c_v)
            ten(lambda q: q.matmul(pb7[0:1, 0:128], smal[:, 0:1], identf, start=True, stop=True))
            w("vector", c_t)
            vec(lambda q: q.tensor_reduce(cc_sb[0:1, 0:1], pb7[0:1, 0:128], AX.X, ALU.max))
            vsync()
            w("gpsimd", c_v)
            dma_g(cc_in[0:1, 0:1], cc_sb[0:1, 0:1])
            w("gpsimd", c_gdma)
            gps(lambda q: q.collective_compute(
                "AllReduce", ALU.max, replica_groups=[[0, 1, 2, 3, 4, 5, 6, 7]],
                ins=[cc_in[0:1, 0:1]], outs=[cc_out[0:1, 0:1]]))
            t_coll = c_g.val

            if phase_limit < 2:
                raise _PhaseStop()
            EVA, EVB = 43008, 44032
            w("sync", c_gdma)
            for tt in range(TT):
                w("sync", c_v)
                t_l = dma_in(xn_t, s_xn[tt * 128:(tt + 1) * 128, :])
                w("vector", c_dma, t_l)
                w("vector", c_v, t_irms)
                w("vector", c_t)
                vec(lambda q, tt=tt: q.tensor_scalar(un_t, xn_t, irms[:, tt:tt + 1], None, ALU.mult))
                t_un = c_v.val
                for half in range(2):
                    w("tensor", c_v)
                    for k in range(half * 4, half * 4 + 4):
                        ten(lambda q, k=k: q.transpose(
                            pbank[4 + (k % 4) % 2][:, ((k % 4) // 2) * 128:((k % 4) // 2) * 128 + 128],
                            un_t[:, k * 128:(k + 1) * 128], identf))
                    w("vector", c_t)
                    w("vector", c_gdma)
                    for k in range(half * 4, half * 4 + 4):
                        vec(lambda q, k=k: q.tensor_scalar(
                            ARB[:, EVA + k * 128:EVA + (k + 1) * 128],
                            pbank[4 + (k % 4) % 2][:, ((k % 4) // 2) * 128:((k % 4) // 2) * 128 + 128],
                            v_wnorm[:, k:k + 1], None, ALU.mult))
                w("gpsimd", c_v)
                for k in range(KC):
                    dma_g(s_un[k, :, tt * 128:(tt + 1) * 128], ARB[:, EVA + k * 128:EVA + (k + 1) * 128])

            w("gpsimd", c_g, t_coll)
            t_ccl = dma_g(cc_sb[0:1, 1:2], cc_out[0:1, 0:1])
            w("vector", c_gdma, t_ccl)
            vec(lambda q: q.tensor_scalar(cc_sb[0:1, 2:3], cc_sb[0:1, 1:2], 1.0 / 127.0, 1e-5, ALU.mult, ALU.max))
            vsync()
            vec(lambda q: q.reciprocal(cc_sb[0:1, 3:4], cc_sb[0:1, 2:3]))
            vsync()
            w("tensor", c_v)
            ten(lambda q: q.matmul(pb6[:, 0:1], onesf, cc_sb[0:1, 3:4], start=True, stop=True))
            ten(lambda q: q.matmul(pb6[:, 1:2], onesf, cc_sb[0:1, 2:3], start=True, stop=True))
            w("vector", c_t)
            vec(lambda q: q.tensor_copy(qscale[:], pb6[:, 0:1]))
            vec(lambda q: q.tensor_copy(smal[:, 1:2], pb6[:, 1:2]))
            vsync()
            vec(lambda q: q.tensor_scalar(fsc2[:], v_fsc[:], smal[:, 1:2], None, ALU.mult))
            vsync()
            t_qs = c_v.val

            if phase_limit < 3:
                raise _PhaseStop()
            for tt in range(TT):
                w("sync", c_v)
                t_l = dma_in(xn_t, s_xn[tt * 128:(tt + 1) * 128, :])
                w("vector", c_dma, t_l)
                w("vector", c_v, t_qs)
                vec(lambda q: q.tensor_scalar(i32scr[:], xn_t, qscale[:], None, ALU.mult))
                vec(lambda q: q.tensor_scalar(i32scr[:], i32scr[:], 127, -128, ALU.min, ALU.max))
                w("vector", c_gdma)
                vec(lambda q: q.tensor_copy(ARB[:, EVA:EVA + DM], i32scr[:]))
                for half in range(2):
                    w("tensor", c_v)
                    for k in range(half * 4, half * 4 + 4):
                        ten(lambda q, k=k: q.matmul(
                            pbank[4 + (k % 4) % 2][:, ((k % 4) // 2) * 128:((k % 4) // 2) * 128 + 128],
                            ARB[:, EVA + k * 128:EVA + (k + 1) * 128], identb, start=True, stop=True))
                    w("vector", c_t)
                    for k in range(half * 4, half * 4 + 4):
                        vec(lambda q, k=k: q.tensor_copy(
                            ARB[:, EVB + k * 128:EVB + (k + 1) * 128],
                            pbank[4 + (k % 4) % 2][:, ((k % 4) // 2) * 128:((k % 4) // 2) * 128 + 128]))
                w("gpsimd", c_v)
                for k in range(KC):
                    dma_g(s_qf[k, :, tt * 128:(tt + 1) * 128], ARB[:, EVB + k * 128:EVB + (k + 1) * 128])

            if phase_limit < 4:
                raise _PhaseStop()
            wstg = ARF[:, 0:4096]
            ZOFF = 40960
            w("sync", c_gdma)
            for k in range(KC):
                dma_g(ARB[:, k * 4096:(k + 1) * 4096], d_w_inT[k * 128:(k + 1) * 128, :])
            t_wcast = c_gdma.val
            if phase_limit < 4.2:
                raise _PhaseStop()
            _lhs = [0] if phase_limit < 4.6 else range(2)
            _ms = [0] if phase_limit < 4.3 else (range(GC) if phase_limit < 4.4 else range(2 * E // 128))
            for lh in _lhs:
                lo = lh * 1024
                w("sync", c_t)
                if phase_limit < 4.22:
                    raise _PhaseStop()
                t_un = 0
                for k in range(KC if phase_limit >= 4.24 else 1):
                    t_un = dma_in(ARB[:, 32768 + k * 1024:32768 + (k + 1) * 1024], s_un[k, :, lo:lo + 1024])
                if phase_limit < 4.25:
                    raise _PhaseStop()
                for m in _ms:
                    for lc in range(2):
                        pb = pbank[(m * 2 + lc) % 4]
                        w("tensor", c_dma, t_un)
                        w("tensor", c_gdma, t_wcast)
                        w("tensor", c_v)
                        w("tensor", c_s)
                        for k in range(KC - 1):
                            ten0(lambda q, k=k, m=m, lc=lc, pb=pb: q.matmul(
                                pb[:, 0:512],
                                ARB[:, k * 4096 + m * 128:k * 4096 + (m + 1) * 128],
                                ARB[:, 32768 + k * 1024 + lc * 512:32768 + k * 1024 + (lc + 1) * 512],
                                start=(k == 0), stop=False))
                        t_mm = ten(lambda q, k=KC - 1, m=m, lc=lc, pb=pb: q.matmul(
                            pb[:, 0:512],
                            ARB[:, k * 4096 + m * 128:k * 4096 + (m + 1) * 128],
                            ARB[:, 32768 + k * 1024 + lc * 512:32768 + k * 1024 + (lc + 1) * 512],
                            start=False, stop=True))
                        if phase_limit < 4.27:
                            continue
                        if m < GC:
                            w("vector", c_t, t_mm)
                            w("vector", c_gdma)
                            t_e = vec(lambda q, pb=pb, lc=lc: q.tensor_copy(
                                ARB[:, ZOFF + lc * 512:ZOFF + (lc + 1) * 512], pb[:, 0:512]))
                            if phase_limit < 4.28:
                                continue
                            w("gpsimd", c_v, t_e)
                            dma_g(s_xr[m, :, lo + lc * 512:lo + (lc + 1) * 512],
                                  ARB[:, ZOFF + lc * 512:ZOFF + (lc + 1) * 512])
                        else:
                            g = m - GC
                            w("scalar", c_t, t_mm)
                            w("scalar", c_gdma)
                            t_e = sca(lambda q, pb=pb, lc=lc: q.activation(
                                out=ARB[:, ZOFF + 1024 + lc * 512:ZOFF + 1024 + (lc + 1) * 512],
                                in_=pb[:, 0:512], func=AF.Silu))
                            w("gpsimd", c_s, t_e)
                            dma_g(s_z[g, :, lo + lc * 512:lo + (lc + 1) * 512],
                                  ARB[:, ZOFF + 1024 + lc * 512:ZOFF + 1024 + (lc + 1) * 512])
            t_phB_g = c_gdma.val

            if phase_limit < 5:
                raise _PhaseStop()
            XIN = 8192
            w("gpsimd", c_t)
            dma_g(ARB[:, 0:4096], d_convd[:, 0:4096])
            dma_g(ARB[:, 4096:8192], d_convd[:, 4096:8192])
            dma_g(ARB[:, 10752:12288], d_w_xT)
            dma_g(ARB[:, XIN:XIN + 4], d_zero4)
            t_wc = c_gdma.val
            for g in range(GC):
                w("sync", c_t)
                t_l = dma_in(ARB[:, XIN + 4:XIN + 4 + L], s_xr[g, :, :])
                w("tensor", c_dma, t_l)
                w("tensor", c_gdma, t_wc)
                w("tensor", c_s)
                w("tensor", c_v)
                for lc in range(4):
                    pb = pbank[lc % 2]
                    if lc >= 2:
                        w("tensor", c_s)
                    for tap in range(3):
                        off = (tap * GC + g) * 128
                        ten0(lambda q, off=off, lc=lc, tap=tap, pb=pb: q.matmul(
                            pb[:, 0:512],
                            ARB[:, off:off + 128],
                            ARB[:, XIN + 1 + lc * 512 + tap:XIN + 1 + lc * 512 + tap + 512],
                            start=(tap == 0), stop=False))
                    _off3 = (3 * GC + g) * 128
                    t_mm = ten(lambda q, off=_off3, lc=lc, pb=pb: q.matmul(
                        pb[:, 0:512],
                        ARB[:, off:off + 128],
                        ARB[:, XIN + 1 + lc * 512 + 3:XIN + 1 + lc * 512 + 3 + 512],
                        start=False, stop=True))
                    w("scalar", c_t, t_mm)
                    w("scalar", c_gdma)
                    t_e = sca(lambda q, pb=pb, lc=lc, g=g: q.activation(
                        out=ARB[:, ZOFF + lc * 512:ZOFF + (lc + 1) * 512], in_=pb[:, 0:512],
                        func=AF.Silu, bias=v_convb[:, g:g + 1]))
                    w("gpsimd", c_s, t_e)
                    dma_g(s_xc[g, :, lc * 512:(lc + 1) * 512], ARB[:, ZOFF + lc * 512:ZOFF + (lc + 1) * 512])

            if phase_limit < 6:
                raise _PhaseStop()
            XCT = 0
            w("sync", c_gdma)
            DTR, BRO, DTR2 = 24576, 26624, 28672
            WDT = 16384
            t_wdt = dma_g(ARB[0:R, WDT:WDT + E], d_w_dtT)
            for lc in range(4):
                w("sync", c_t)
                t_x = 0
                for g in range(GC):
                    t_x = dma_in(ARB[:, XCT + g * 512:XCT + (g + 1) * 512], s_xc[g, :, lc * 512:(lc + 1) * 512])
                pb = pbank[4 + lc % 2]
                w("tensor", c_dma, t_x)
                w("tensor", c_v)
                for g in range(GC - 1):
                    ten0(lambda q, g=g, pb=pb: q.matmul(
                        pb[0:96, 0:512],
                        ARB[:, 10752 + g * 96:10752 + (g + 1) * 96],
                        ARB[:, XCT + g * 512:XCT + (g + 1) * 512],
                        start=(g == 0), stop=False))
                t_mm = ten(lambda q, g=GC - 1, pb=pb: q.matmul(
                    pb[0:96, 0:512],
                    ARB[:, 10752 + g * 96:10752 + (g + 1) * 96],
                    ARB[:, XCT + g * 512:XCT + (g + 1) * 512],
                    start=False, stop=True))
                w("vector", c_t, t_mm)
                vec(lambda q, pb=pb, lc=lc: q.tensor_copy(ARB[0:96, DTR + lc * 512:DTR + (lc + 1) * 512],
                                                          pb[0:96, 0:512]))
            w("gpsimd", c_v)
            t_rows = dma_g(ARB[0:32, BRO:BRO + L], ARB[64:96, DTR:DTR + L])
            t_rows = dma_g(ARB[0:64, DTR2:DTR2 + L], ARB[0:64, DTR:DTR + L])
            if True:
                pass

            if phase_limit < 7:
                raise _PhaseStop()
            if debug:
                w("gpsimd", c_v)
                dma_g(s_dbg2[:, 0:L], ARB[:, DTR:DTR + L])
                w("gpsimd", c_gdma)
                dma_g(s_dbg2[0:32, L:2 * L], ARB[0:32, BRO:BRO + L])
            dtg = ARF[:, 4096:6144]
            dtg2 = ARF[:, 6144:8192]
            w("tensor", c_gdma)
            for g in range(GC):
                w("tensor", c_v)
                for lc in range(4):
                    ten(lambda q, g=g, lc=lc: q.matmul(
                        pbank[lc % 4][:, 0:512],
                        ARB[0:R, WDT + g * 128:WDT + (g + 1) * 128],
                        ARB[0:R, DTR2 + lc * 512:DTR2 + (lc + 1) * 512],
                        start=True, stop=True))
                t_mm = c_t.val
                w("vector", c_t, t_mm)
                w("vector", c_gdma)
                w("vector", c_s)
                for lc in range(4):
                    vec(lambda q, g=g, lc=lc: q.tensor_scalar(
                        dtg[:, lc * 512:(lc + 1) * 512], pbank[lc % 4][:, 0:512],
                        v_bdt[:, g:g + 1], None, ALU.add))
                if debug and g == 0:
                    w("gpsimd", c_v)
                    dma_g(s_dbg3[:, 0:L], dtg)
                w("scalar", c_v)
                sca(lambda q: q.activation(out=dtg2, in_=dtg, func=AF.Abs))
                sca(lambda q: q.activation(out=dtg2, in_=dtg2, func=AF.Exp, scale=-1.0))
                sca(lambda q: q.activation(out=dtg2, in_=dtg2, func=AF.Ln, bias=1.0))
                w("vector", c_s)
                vec(lambda q: q.scalar_tensor_tensor(dtg, dtg, 0.0, dtg2, ALU.max, ALU.add))
                w("gpsimd", c_v)
                dma_g(s_dt[g, :, :], dtg)

            if phase_limit < 8:
                raise _PhaseStop()
            XTG, U2G, DBU, SNG, SCI = 0, 2048, 4096, 6144, 8192
            YTM = DBU
            BCB, BCC = 30720, 47104
            dAg = ARF[:, 8192:10240]
            yg = ARF[:, 10240:12288]
            yg2 = ARF[:, 12288:14336]
            w("sync", c_t)
            w("sync", c_gdma)
            for oct_ in range(2):
                if oct_ == 0:
                    w("tensor", c_gdma, t_rows)
                for i in range(NOCT):
                    n = oct_ * NOCT + i
                    for (srow, dstoff) in ((n, BCB), (16 + n, BCC)):
                        for lc in range(4):
                            pb = pbank[4 + (i * 8 + lc) % 2]
                            w("tensor", c_v)
                            ten(lambda q, srow=srow, lc=lc, pb=pb: q.matmul(
                                pb[:, 0:512], sel32[:, srow * 128:(srow + 1) * 128],
                                ARB[0:32, BRO + lc * 512:BRO + (lc + 1) * 512],
                                start=True, stop=True))
                            w("vector", c_t)
                            vec(lambda q, i=i, dstoff=dstoff, lc=lc, pb=pb: q.tensor_copy(
                                ARB[:, dstoff + i * 2048 + lc * 512:dstoff + i * 2048 + (lc + 1) * 512],
                                pb[:, 0:512]))
                t_bc = c_v.val

                if oct_ == 1:
                    w("sync", c_gdma)
                for g in range(GC):
                    w("sync", c_v)
                    t_x = dma_in(ARB[:, XTG:XTG + L], s_xc[g, :, :])
                    w("sync", c_s)
                    t_dtl = dma_in(dtg, s_dt[g, :, :])
                    w("vector", c_dma, t_dtl)
                    vec(lambda q: q.tensor_tensor(ARB[:, U2G:U2G + L], dtg, ARB[:, XTG:XTG + L], ALU.mult))
                    t_u2 = c_v.val
                    for i in range(NOCT):
                        n = oct_ * NOCT + i
                        w("scalar", c_v)
                        w("scalar", c_dma, t_dtl)
                        t_dA = sca(lambda q, g=g, n=n: q.activation(
                            out=dAg, in_=dtg, func=AF.Exp, scale=v_negA[:, g * N + n:g * N + n + 1]))
                        vec(lambda q, i=i: q.tensor_tensor(
                            ARB[:, DBU:DBU + L], ARB[:, U2G:U2G + L],
                            ARB[:, BCB + i * 2048:BCB + (i + 1) * 2048], ALU.mult))
                        w("vector", c_s, t_dA)
                        vec(lambda q: q.tensor_tensor_scan(
                            ARB[:, SNG:SNG + L], dAg, ARB[:, DBU:DBU + L], 0.0, ALU.mult, ALU.add))
                        vec(lambda q, i=i: q.tensor_tensor(
                            ARB[:, SCI + i:SCI + i + 8 * L:8],
                            ARB[:, SNG:SNG + L],
                            ARB[:, BCC + i * 2048:BCC + (i + 1) * 2048], ALU.mult))
                    w("vector", c_gdma)
                    t_yo = vec(lambda q: q.tensor_reduce(
                        yg2, ARB[:, SCI:SCI + 8 * L].rearrange("p (l o) -> p l o", o=8), AX.X, ALU.add))
                    if oct_ == 0:
                        w("gpsimd", c_v, t_yo)
                        dma_g(s_y0[g, :, :], yg2)
                    else:
                        w("sync", c_v)
                        t_y0 = dma_in(yg, s_y0[g, :, :])
                        t_z = dma_in(ARB[:, YTM:YTM + L], s_z[g, :, :])
                        w("vector", c_dma, t_z)
                        vec(lambda q: q.tensor_tensor(yg, yg, yg2, ALU.add))
                        vec(lambda q, g=g: q.scalar_tensor_tensor(
                            yg, ARB[:, XTG:XTG + L], v_dpar[:, g:g + 1], yg, ALU.mult, ALU.add))
                        w("vector", c_gdma)
                        t_y2 = vec(lambda q: q.tensor_tensor(
                            ARB[:, SNG:SNG + L], yg, ARB[:, YTM:YTM + L], ALU.mult))
                        w("gpsimd", c_v, t_y2)
                        dma_g(s_y2[g, :, :], ARB[:, SNG:SNG + L])
            t_phE_g = c_gdma.val

            if phase_limit < 9:
                raise _PhaseStop()
            Y2T, QFT = 30720, 38912
            outf = ARF[:, 4096:12288]
            evacf = ARF[:, 17408:17920]
            xnH = ARF[:, 14336:15360]
            uH = ARF[:, 15360:16384]
            oH = ARF[:, 16384:17408]
            w("gpsimd", c_v)
            for g in range(GC):
                dma_g(ARB[:, g * 1024:(g + 1) * 1024], d_w_outT[g * 128:(g + 1) * 128, :])
            for k in range(KC):
                dma_g(ARB[:, 16384 + k * 1024:16384 + (k + 1) * 1024], d_ffn_wT[k * 128:(k + 1) * 128, :])
            t_phF_w = c_gdma.val

            for lh in range(2):
                lo = lh * 1024
                for lc in range(2):
                    l0 = lo + lc * 512
                    w("sync", c_t)
                    w("sync", c_gdma, t_phF_w)
                    t_y2 = 0
                    for g in range(GC):
                        t_y2 = dma_in(ARB[:, Y2T + g * 512:Y2T + (g + 1) * 512], s_y2[g, :, l0:l0 + 512])
                    w("tensor", c_dma, t_y2)
                    w("tensor", c_gdma, t_phF_w)
                    for m in range(KC):
                        pb = pbank[m % 4]
                        w("tensor", c_v)
                        for g in range(GC - 1):
                            ten0(lambda q, g=g, m=m, pb=pb: q.matmul(
                                pb[:, 0:512],
                                ARB[:, g * 1024 + m * 128:g * 1024 + (m + 1) * 128],
                                ARB[:, Y2T + g * 512:Y2T + (g + 1) * 512],
                                start=(g == 0), stop=False))
                        t_mm = ten(lambda q, g=GC - 1, m=m, pb=pb: q.matmul(
                            pb[:, 0:512],
                            ARB[:, g * 1024 + m * 128:g * 1024 + (m + 1) * 128],
                            ARB[:, Y2T + g * 512:Y2T + (g + 1) * 512],
                            start=False, stop=True))
                        w("vector", c_t, t_mm)
                        vec(lambda q, m=m, lc=lc, pb=pb: q.tensor_copy(
                            outf[:, m * 1024 + lc * 512:m * 1024 + (lc + 1) * 512], pb[:, 0:512]))
                    w("sync", c_t)
                    t_qf = 0
                    for k in range(KC):
                        t_qf = dma_in(ARB[:, QFT + k * 512:QFT + (k + 1) * 512], s_qf[k, :, l0:l0 + 512])
                    w("tensor", c_dma, t_qf)
                    for m in range(KC):
                        pb = pbank[4 + m % 2]
                        w("tensor", c_s)
                        w("tensor", c_v)
                        for k in range(KC - 1):
                            ten0(lambda q, k=k, m=m, pb=pb: q.matmul(
                                pb[:, 0:512],
                                ARB[:, 16384 + k * 1024 + m * 128:16384 + k * 1024 + (m + 1) * 128],
                                ARB[:, QFT + k * 512:QFT + (k + 1) * 512],
                                start=(k == 0), stop=False))
                        t_mm = ten(lambda q, k=KC - 1, m=m, pb=pb: q.matmul(
                            pb[:, 0:512],
                            ARB[:, 16384 + k * 1024 + m * 128:16384 + k * 1024 + (m + 1) * 128],
                            ARB[:, QFT + k * 512:QFT + (k + 1) * 512],
                            start=False, stop=True))
                        w("scalar", c_t, t_mm)
                        w("scalar", c_v)
                        t_ffn = sca(lambda q, m=m, pb=pb: q.activation(
                            out=evacf[:, 0:512], in_=pb[:, 0:512],
                            func=AF.Identity, bias=v_fbias[:, m:m + 1], scale=fsc2[:, m:m + 1]))
                        w("vector", c_s, t_ffn)
                        vec(lambda q, m=m, lc=lc: q.tensor_tensor(
                            outf[:, m * 1024 + lc * 512:m * 1024 + (lc + 1) * 512],
                            outf[:, m * 1024 + lc * 512:m * 1024 + (lc + 1) * 512],
                            evacf[:, 0:512], ALU.add))

                for ti in range(8):
                    tt = lh * 8 + ti
                    w("sync", c_v)
                    t_u = dma_in(uH, d_u[tt * 128:(tt + 1) * 128, :])
                    t_x = dma_in(xnH, s_xn[tt * 128:(tt + 1) * 128, :])
                    w("tensor", c_v)
                    for k in range(KC):
                        ten(lambda q, k=k, ti=ti: q.transpose(
                            pbank[k % 4][:, (k // 4) * 128:(k // 4) * 128 + 128],
                            outf[:, k * 1024 + ti * 128:k * 1024 + ti * 128 + 128], identf))
                    t_tr = c_t.val
                    w("vector", c_t, t_tr)
                    w("vector", c_dma, t_x)
                    w("vector", c_gdma)
                    for k in range(KC):
                        vec(lambda q, k=k: q.tensor_tensor(
                            oH[:, k * 128:(k + 1) * 128],
                            pbank[k % 4][:, (k // 4) * 128:(k // 4) * 128 + 128],
                            uH[:, k * 128:(k + 1) * 128], ALU.add))
                    t_o = vec(lambda q: q.tensor_tensor(oH, oH, xnH, ALU.add))
                    w("gpsimd", c_v, t_o)
                    dma_g(d_out[tt * 128:(tt + 1) * 128, :], oH)


        except _PhaseStop:
            pass

        emit("gpsimd", lambda q, t=c_gdma.val: q.wait_ge(sm_gdma, t))

        @block.sync
        def _(q):
            for fn in streams["sync"]:
                fn(q)

        @block.scalar
        def _(q):
            for fn in streams["scalar"]:
                fn(q)

        @block.vector
        def _(q):
            for fn in streams["vector"]:
                fn(q)

        @block.tensor
        def _(q):
            for fn in streams["tensor"]:
                fn(q)

        @block.gpsimd
        def _(q):
            for fn in streams["gpsimd"]:
                fn(q)

    return nc


_NC_CACHE = {}


def _get_nc(debug=False):
    if debug not in _NC_CACHE:
        _NC_CACHE[debug] = build_program(debug)
    return _NC_CACHE[debug]


def _sel32():
    sel = np.zeros((32, 32 * 128), dtype=ml_dtypes.bfloat16)
    for t in range(32):
        sel[t, t * 128:(t + 1) * 128] = 1
    return sel


def make_inputs(L0_attn_u, L0_attn_w_norm, L0_attn_w_in, L0_attn_conv_w, L0_attn_conv_b,
                L0_attn_w_x, L0_attn_w_dt, L0_attn_b_dt, L0_attn_A_log, L0_attn_D_param,
                L0_attn_w_out, L0_ffn_weight_int8, L0_ffn_weight_scale, L0_ffn_bias):
    u = np.ascontiguousarray(np.asarray(L0_attn_u, dtype=np.float32))
    w_inT = np.ascontiguousarray(np.asarray(L0_attn_w_in, dtype=np.float32).T)
    w_xT = np.ascontiguousarray(np.asarray(L0_attn_w_x, dtype=np.float32).T.reshape(GC, 128, 96).transpose(1, 0, 2).reshape(128, GC * 96))
    w_dtT = np.ascontiguousarray(np.asarray(L0_attn_w_dt, dtype=np.float32).T)
    w_outT = np.ascontiguousarray(np.asarray(L0_attn_w_out, dtype=np.float32).T)
    conv_w = np.asarray(L0_attn_conv_w, dtype=np.float32)
    ffn_wT = np.ascontiguousarray(np.asarray(L0_ffn_weight_int8).T)

    convd = np.zeros((4 * GC, 128, 128), np.float32)
    idx = np.arange(128)
    for tap in range(4):
        for g in range(GC):
            convd[tap * GC + g, idx, idx] = conv_w[g * 128:(g + 1) * 128, 0, tap]
    convd = np.ascontiguousarray(convd.transpose(1, 0, 2).reshape(128, 4 * GC * 128))

    def cols(v, k):
        return np.ascontiguousarray(np.asarray(v, dtype=np.float32).reshape(k, 128).T)

    alog = np.asarray(L0_attn_A_log, dtype=np.float32).reshape(GC, 128, N)
    shared = {
        "w_inT": w_inT, "w_xT": w_xT, "w_dtT": w_dtT, "w_outT": w_outT,
        "convd": convd, "ffn_wT": ffn_wT,
        "wnorm_c": cols(L0_attn_w_norm, KC),
        "convb_c": cols(L0_attn_conv_b, GC),
        "bdt_c": cols(L0_attn_b_dt, GC),
        "dpar_c": cols(L0_attn_D_param, GC),
        "alog_c": np.ascontiguousarray(alog.transpose(1, 0, 2).reshape(128, GC * N)),
        "fscale_c": cols(np.asarray(L0_ffn_weight_scale, np.float32).reshape(-1), KC),
        "fbias_c": cols(L0_ffn_bias, KC),
        "identf": np.eye(128, dtype=np.float32),
        "identb_in": np.eye(128, dtype=ml_dtypes.bfloat16),
        "zero4": np.zeros((128, 4), dtype=ml_dtypes.bfloat16),
        "sel32": _sel32(),
    }
    return [dict(shared, u=np.ascontiguousarray(u[b])) for b in range(8)]


def kernel(**inputs):
    in_maps = make_inputs(**inputs)
    nc = _get_nc(False)
    res = run_bass_kernel_spmd(nc, in_maps, core_ids=list(range(8)))
    out = np.stack([res.results[b]["out"] for b in range(8)], axis=0)
    return out.astype(np.float32)


# version 03
# speedup vs baseline: 1.5100x; 1.5100x over previous
import numpy as np
import ml_dtypes
import concourse.bass as bass
import concourse.mybir as mybir
from concourse.bass_utils import run_bass_kernel_spmd

F32 = mybir.dt.float32
BF16 = mybir.dt.bfloat16
I8 = mybir.dt.int8
I32 = mybir.dt.int32
AF = mybir.ActivationFunctionType
ALU = mybir.AluOpType
AX = mybir.AxisListType

L = 2048
DM = 1024
E = 2048
N = 16
R = 64
KC = DM // 128
GC = E // 128
TT = L // 128
EPS = 1e-5
NOCT = 8

class _PhaseStop(Exception):
    pass


AB_COLS = 63488
AF_COLS = 18432


def build_program(debug=False, phase_limit=99):
    nc = bass.Bass("TRN2", target_bir_lowering=False, debug=False)

    kind_scr = "ExternalOutput" if debug else "Internal"

    d_u = nc.dram_tensor("u", [L, DM], F32, kind="ExternalInput").ap()
    d_w_inT = nc.dram_tensor("w_inT", [DM, 2 * E], F32, kind="ExternalInput").ap()
    d_w_xT = nc.dram_tensor("w_xT", [128, GC * 96], F32, kind="ExternalInput").ap()
    d_w_dtT = nc.dram_tensor("w_dtT", [R, E], F32, kind="ExternalInput").ap()
    d_w_outT = nc.dram_tensor("w_outT", [E, DM], F32, kind="ExternalInput").ap()
    d_convd = nc.dram_tensor("convd", [128, 4 * GC * 128], F32, kind="ExternalInput").ap()
    d_ffn_wT = nc.dram_tensor("ffn_wT", [DM, DM], I8, kind="ExternalInput").ap()
    d_wnorm = nc.dram_tensor("wnorm_c", [128, KC], F32, kind="ExternalInput").ap()
    d_convb = nc.dram_tensor("convb_c", [128, GC], F32, kind="ExternalInput").ap()
    d_bdt = nc.dram_tensor("bdt_c", [128, GC], F32, kind="ExternalInput").ap()
    d_Dpar = nc.dram_tensor("dpar_c", [128, GC], F32, kind="ExternalInput").ap()
    d_Alog = nc.dram_tensor("alog_c", [128, GC * N], F32, kind="ExternalInput").ap()
    d_fscale = nc.dram_tensor("fscale_c", [128, KC], F32, kind="ExternalInput").ap()
    d_fbias = nc.dram_tensor("fbias_c", [128, KC], F32, kind="ExternalInput").ap()
    d_identf = nc.dram_tensor("identf", [128, 128], F32, kind="ExternalInput").ap()
    d_sel = nc.dram_tensor("sel32", [32, 32 * 128], BF16, kind="ExternalInput").ap()
    d_identb = nc.dram_tensor("identb_in", [128, 128], BF16, kind="ExternalInput").ap()
    d_zero4 = nc.dram_tensor("zero4", [128, 4], BF16, kind="ExternalInput").ap()

    d_out = nc.dram_tensor("out", [L, DM], F32, kind="ExternalOutput").ap()

    s_xn = nc.dram_tensor("s_xn", [L, DM], F32, kind=kind_scr).ap()
    s_un = nc.dram_tensor("s_un", [KC, 128, L], BF16, kind=kind_scr).ap()
    s_xr = nc.dram_tensor("s_xr", [GC, 128, L], BF16, kind=kind_scr).ap()
    s_xc = nc.dram_tensor("s_xc", [GC, 128, L], BF16, kind=kind_scr).ap()
    s_z = nc.dram_tensor("s_z", [GC, 128, L], BF16, kind=kind_scr).ap()
    s_dt = nc.dram_tensor("s_dt", [GC, 128, L], F32, kind=kind_scr).ap()
    s_y0 = nc.dram_tensor("s_y0", [GC, 128, L], F32, kind="Internal").ap()
    s_y2 = nc.dram_tensor("s_y2", [GC, 128, L], BF16, kind=kind_scr).ap()
    s_qf = nc.dram_tensor("s_qf", [KC, 128, L], BF16, kind=kind_scr).ap()
    s_dbg = nc.dram_tensor("s_dbg", [128, 112], F32, kind=kind_scr).ap()
    s_dbg2 = nc.dram_tensor("s_dbg2", [128, 2 * L], BF16, kind=kind_scr).ap()
    s_dbg3 = nc.dram_tensor("s_dbg3", [128, L], F32, kind=kind_scr).ap()
    cc_in = nc.dram_tensor("cc_in", [1, 16], F32, kind="Internal").ap()
    cc_out = nc.dram_tensor("cc_out", [1, 16], F32, kind="Internal", addr_space="Shared").ap()

    streams = {"sync": [], "scalar": [], "vector": [], "tensor": [], "gpsimd": []}

    def emit(eng, fn):
        streams[eng].append(fn)

    class Ctr:
        def __init__(self, sem):
            self.sem = sem
            self.val = 0

    import contextlib
    with contextlib.ExitStack() as _stack:
        _e = _stack.enter_context
        v_wnorm = _e(nc.sbuf_tensor("v_wnorm", [128, KC], F32))
        v_convb = _e(nc.sbuf_tensor("v_convb", [128, GC], F32))
        v_bdt = _e(nc.sbuf_tensor("v_bdt", [128, GC], F32))
        v_dpar = _e(nc.sbuf_tensor("v_dpar", [128, GC], F32))
        v_negA = _e(nc.sbuf_tensor("v_negA", [128, GC * N], F32))
        v_fsc = _e(nc.sbuf_tensor("v_fsc", [128, KC], F32))
        v_fbias = _e(nc.sbuf_tensor("v_fbias", [128, KC], F32))
        identf_ = _e(nc.sbuf_tensor("sb_identf", [128, 128], F32))
        identb_ = _e(nc.sbuf_tensor("identb", [128, 128], BF16))
        onesb_ = _e(nc.sbuf_tensor("onesb", [1, 128], BF16))
        onesf_ = _e(nc.sbuf_tensor("onesf", [1, 128], F32))
        stats_ = _e(nc.sbuf_tensor("stats", [128, 64], F32))
        mcol_ = _e(nc.sbuf_tensor("mcol", [128, TT], F32))
        iscol_ = _e(nc.sbuf_tensor("iscol", [128, TT], F32))
        irms_ = _e(nc.sbuf_tensor("irms", [128, TT], F32))
        smal_ = _e(nc.sbuf_tensor("smal", [128, 16], F32))
        qscale_ = _e(nc.sbuf_tensor("qscale", [128, 1], F32))
        fsc2_ = _e(nc.sbuf_tensor("fsc2", [128, KC], F32))
        cc_sb_ = _e(nc.sbuf_tensor("cc_sb", [1, 16], F32))
        sel32_ = _e(nc.sbuf_tensor("sb_sel32", [32, 32 * 128], BF16))
        ARF = _e(nc.sbuf_tensor("ARF", [128, AF_COLS], F32))
        ARB = _e(nc.sbuf_tensor("ARB", [128, AB_COLS], BF16))
        pb0 = _e(nc.psum_tensor("pb0", [128, 512], F32))
        pb1 = _e(nc.psum_tensor("pb1", [128, 512], F32))
        pb2 = _e(nc.psum_tensor("pb2", [128, 512], F32))
        pb3 = _e(nc.psum_tensor("pb3", [128, 512], F32))
        pb4 = _e(nc.psum_tensor("pb4", [128, 512], F32))
        pb5 = _e(nc.psum_tensor("pb5", [128, 512], F32))
        pb6 = _e(nc.psum_tensor("pb6", [128, 512], F32))
        pb7 = _e(nc.psum_tensor("pb7", [128, 512], F32))
        sm_dma = _e(nc.semaphore("sm_dma"))
        sm_gdma = _e(nc.semaphore("sm_gdma"))
        sm_t = _e(nc.semaphore("sm_t"))
        sm_v = _e(nc.semaphore("sm_v"))
        sm_s = _e(nc.semaphore("sm_s"))
        sm_g = _e(nc.semaphore("sm_g"))
        block = _e(nc.Block())
        identf = identf_[:]
        identb = identb_[:]
        onesb = onesb_[:]
        onesf = onesf_[:]
        stats = stats_
        mcol, iscol, irms = mcol_, iscol_, irms_
        smal, qscale, fsc2, cc_sb = smal_, qscale_, fsc2_, cc_sb_
        i32scr = ARF[:, 16384:17408].bitcast(I32)
        stg8 = ARF[:, 17920:18432].bitcast(I8)[:, 0:DM]
        sel32 = sel32_
        pbank = [pb0, pb1, pb2, pb3, pb4, pb5, pb6, pb7]

        c_dma, c_gdma = Ctr(sm_dma), Ctr(sm_gdma)
        c_t, c_v, c_s, c_g = Ctr(sm_t), Ctr(sm_v), Ctr(sm_s), Ctr(sm_g)

        def dma_in(dst, src):
            c_dma.val += 16
            emit("sync", lambda q, d=dst, s=src: q.dma_start(out=d, in_=s).then_inc(sm_dma, 16))
            return c_dma.val

        def dma_g(dst, src):
            c_gdma.val += 16
            emit("gpsimd", lambda q, d=dst, s=src: q.dma_start(out=d, in_=s).then_inc(sm_gdma, 16))
            return c_gdma.val

        def w(eng, ctr, tgt=None):
            t = ctr.val if tgt is None else tgt
            if t > 0:
                emit(eng, lambda q, c=ctr, tv=t: q.wait_ge(c.sem, tv))

        def ten(fn):
            c_t.val += 1
            emit("tensor", lambda q, f=fn: f(q).then_inc(sm_t, 1))
            return c_t.val

        def ten0(fn):
            emit("tensor", lambda q, f=fn: f(q))

        def ten_mark():
            c_t.val += 1
            emit("tensor", lambda q: q.nop().then_inc(sm_t, 1))
            return c_t.val

        def vec(fn):
            c_v.val += 1
            emit("vector", lambda q, f=fn: f(q).then_inc(sm_v, 1))
            return c_v.val

        def vsync():
            emit("vector", lambda q, tv=c_v.val: q.wait_ge(sm_v, tv))

        def sca(fn):
            c_s.val += 1
            emit("scalar", lambda q, f=fn: f(q).then_inc(sm_s, 1))
            return c_s.val

        def gps(fn):
            c_g.val += 1
            emit("gpsimd", lambda q, f=fn: f(q).then_inc(sm_g, 1))
            return c_g.val

        try:
            for dst, src in ((v_wnorm[:], d_wnorm), (v_convb[:], d_convb), (v_bdt[:], d_bdt),
                             (v_dpar[:], d_Dpar), (v_negA[:], d_Alog), (v_fsc[:], d_fscale),
                             (v_fbias[:], d_fbias), (identf, d_identf), (sel32[:], d_sel),
                             (identb, d_identb)):
                dma_in(dst, src)
            w("vector", c_dma)
            vec(lambda q: q.memset(onesb, 1.0))
            vec(lambda q: q.memset(onesf, 1.0))
            w("scalar", c_dma)
            sca(lambda q: q.activation(out=v_negA[:], in_=v_negA[:], func=AF.Exp))
            w("vector", c_s)
            vec(lambda q: q.tensor_scalar(v_negA[:], v_negA[:], -1.0, None, ALU.mult))

            u4 = ARF[:, 0:4096]
            xn_t = ARF[:, 4096:5120]
            un_t = ARF[:, 5120:6144]
            scr = ARF[:, 6144:7168]
            for b in range(4):
                loads = []
                for i in range(4):
                    tt = b * 4 + i
                    w("sync", c_s)
                    w("sync", c_v)
                    loads.append(dma_in(u4[:, i * DM:(i + 1) * DM], d_u[tt * 128:(tt + 1) * 128, :]))
                for i in range(4):
                    tt = b * 4 + i
                    ut = u4[:, i * DM:(i + 1) * DM]
                    w("vector", c_dma, loads[i])
                    w("vector", c_s)
                    vec(lambda q, ut=ut, tt=tt: q.tensor_scalar(
                        scr, ut, 1.0, 0.0, ALU.mult, ALU.add, accum_out=stats[:, tt:tt + 1]))
                    w("scalar", c_dma, loads[i])
                    w("scalar", c_v)
                    sca(lambda q, ut=ut, tt=tt: q.activation(
                        out=scr, in_=ut, func=AF.Square, accum_out=stats[:, 16 + tt:16 + tt + 1]))
            def rsqrt_nr(col, y, t, n=6):
                vec(lambda q, y=y: q.memset(y, 1.0))
                vsync()
                for _ in range(n):
                    vec(lambda q, col=col, y=y, t=t: q.tensor_tensor(t, col, y, ALU.mult))
                    vsync()
                    vec(lambda q, y=y, t=t: q.tensor_tensor(t, t, y, ALU.mult))
                    vsync()
                    vec(lambda q, t=t: q.tensor_scalar(t, t, -0.5, 1.5, ALU.mult, ALU.add))
                    vsync()
                    vec(lambda q, y=y, t=t: q.tensor_tensor(y, y, t, ALU.mult))
                    vsync()
                vec(lambda q, col=col, y=y: q.tensor_copy(col, y))
                vsync()

            w("vector", c_s)
            vsync()
            vec(lambda q: q.tensor_scalar(mcol[:], stats[:, 0:16], 1.0 / DM, None, ALU.mult))
            vec(lambda q: q.tensor_scalar(iscol[:], stats[:, 16:32], 1.0 / DM, None, ALU.mult))
            vsync()
            vec(lambda q: q.tensor_tensor(smal[:, 0:16], mcol[:], mcol[:], ALU.mult))
            vsync()
            vec(lambda q: q.tensor_tensor(iscol[:], iscol[:], smal[:, 0:16], ALU.subtract))
            vsync()
            vec(lambda q: q.tensor_scalar(iscol[:], iscol[:], EPS, None, ALU.add))
            vsync()
            rsqrt_nr(iscol[:], smal[:, 0:16], irms[:])

            for b in range(4):
                loads = []
                for i in range(4):
                    tt = b * 4 + i
                    w("sync", c_s)
                    w("sync", c_v)
                    loads.append(dma_in(u4[:, i * DM:(i + 1) * DM], d_u[tt * 128:(tt + 1) * 128, :]))
                for i in range(4):
                    tt = b * 4 + i
                    ut = u4[:, i * DM:(i + 1) * DM]
                    w("vector", c_dma, loads[i])
                    w("vector", c_gdma)
                    w("vector", c_s)
                    vec(lambda q, ut=ut, tt=tt: q.tensor_scalar(
                        xn_t, ut, mcol[:, tt:tt + 1], iscol[:, tt:tt + 1], ALU.subtract, ALU.mult))
                    vec(lambda q, tt=tt: q.tensor_reduce(
                        stats[:, 32 + tt:32 + tt + 1], xn_t, AX.X, ALU.max, apply_absolute_value=True))
                    w("scalar", c_v)
                    sca(lambda q, tt=tt: q.activation(
                        out=scr, in_=xn_t, func=AF.Square, accum_out=stats[:, 48 + tt:48 + tt + 1]))
                    w("gpsimd", c_v)
                    dma_g(s_xn[tt * 128:(tt + 1) * 128, :], xn_t)

            w("vector", c_s)
            vsync()
            vec(lambda q: q.tensor_scalar(irms[:], stats[:, 48:64], 1.0 / DM, None, ALU.mult))
            vsync()
            vec(lambda q: q.tensor_scalar(irms[:], irms[:], EPS, None, ALU.add))
            vsync()
            rsqrt_nr(irms[:], smal[:, 0:16], mcol[:])
            t_irms = c_v.val

            if debug:
                w("gpsimd", c_v)
                dma_g(s_dbg[:, 0:64], stats[:])
                dma_g(s_dbg[:, 64:80], mcol[:])
                dma_g(s_dbg[:, 80:96], iscol[:])
                dma_g(s_dbg[:, 96:112], irms[:])
            vsync()
            vec(lambda q: q.tensor_reduce(smal[:, 0:1], stats[:, 32:48], AX.X, ALU.max))
            vsync()
            w("tensor", c_v)
            ten(lambda q: q.matmul(pb7[0:1, 0:128], smal[:, 0:1], identf, start=True, stop=True))
            w("vector", c_t)
            vec(lambda q: q.tensor_reduce(cc_sb[0:1, 0:1], pb7[0:1, 0:128], AX.X, ALU.max))
            vsync()
            w("gpsimd", c_v)
            dma_g(cc_in[0:1, 0:1], cc_sb[0:1, 0:1])
            w("gpsimd", c_gdma)
            gps(lambda q: q.collective_compute(
                "AllReduce", ALU.max, replica_groups=[[0, 1, 2, 3, 4, 5, 6, 7]],
                ins=[cc_in[0:1, 0:1]], outs=[cc_out[0:1, 0:1]]))
            t_coll = c_g.val

            if phase_limit < 2:
                raise _PhaseStop()
            EVA, EVB = 43008, 44032
            w("sync", c_gdma)
            for tt in range(TT):
                w("sync", c_v)
                t_l = dma_in(xn_t, s_xn[tt * 128:(tt + 1) * 128, :])
                w("vector", c_dma, t_l)
                w("vector", c_v, t_irms)
                w("vector", c_t)
                vec(lambda q, tt=tt: q.tensor_scalar(un_t, xn_t, irms[:, tt:tt + 1], None, ALU.mult))
                t_un = c_v.val
                for half in range(2):
                    w("tensor", c_v)
                    for k in range(half * 4, half * 4 + 4):
                        ten(lambda q, k=k: q.transpose(
                            pbank[4 + (k % 4) % 2][:, ((k % 4) // 2) * 128:((k % 4) // 2) * 128 + 128],
                            un_t[:, k * 128:(k + 1) * 128], identf))
                    w("vector", c_t)
                    w("vector", c_gdma)
                    for k in range(half * 4, half * 4 + 4):
                        vec(lambda q, k=k: q.tensor_scalar(
                            ARB[:, EVA + k * 128:EVA + (k + 1) * 128],
                            pbank[4 + (k % 4) % 2][:, ((k % 4) // 2) * 128:((k % 4) // 2) * 128 + 128],
                            v_wnorm[:, k:k + 1], None, ALU.mult))
                w("gpsimd", c_v)
                for k in range(KC):
                    dma_g(s_un[k, :, tt * 128:(tt + 1) * 128], ARB[:, EVA + k * 128:EVA + (k + 1) * 128])

            w("gpsimd", c_g, t_coll)
            t_ccl = dma_g(cc_sb[0:1, 1:2], cc_out[0:1, 0:1])
            w("vector", c_gdma, t_ccl)
            vec(lambda q: q.tensor_scalar(cc_sb[0:1, 2:3], cc_sb[0:1, 1:2], 1.0 / 127.0, 1e-5, ALU.mult, ALU.max))
            vsync()
            vec(lambda q: q.reciprocal(cc_sb[0:1, 3:4], cc_sb[0:1, 2:3]))
            vsync()
            w("tensor", c_v)
            ten(lambda q: q.matmul(pb6[:, 0:1], onesf, cc_sb[0:1, 3:4], start=True, stop=True))
            ten(lambda q: q.matmul(pb6[:, 1:2], onesf, cc_sb[0:1, 2:3], start=True, stop=True))
            w("vector", c_t)
            vec(lambda q: q.tensor_copy(qscale[:], pb6[:, 0:1]))
            vec(lambda q: q.tensor_copy(smal[:, 1:2], pb6[:, 1:2]))
            vsync()
            vec(lambda q: q.tensor_scalar(fsc2[:], v_fsc[:], smal[:, 1:2], None, ALU.mult))
            vsync()
            t_qs = c_v.val

            if phase_limit < 3:
                raise _PhaseStop()
            for tt in range(TT):
                w("sync", c_v)
                t_l = dma_in(xn_t, s_xn[tt * 128:(tt + 1) * 128, :])
                w("vector", c_dma, t_l)
                w("vector", c_v, t_qs)
                vec(lambda q: q.tensor_scalar(i32scr[:], xn_t, qscale[:], None, ALU.mult))
                vec(lambda q: q.tensor_scalar(i32scr[:], i32scr[:], 127, -128, ALU.min, ALU.max))
                w("vector", c_gdma)
                vec(lambda q: q.tensor_copy(ARB[:, EVA:EVA + DM], i32scr[:]))
                for half in range(2):
                    w("tensor", c_v)
                    for k in range(half * 4, half * 4 + 4):
                        ten(lambda q, k=k: q.matmul(
                            pbank[4 + (k % 4) % 2][:, ((k % 4) // 2) * 128:((k % 4) // 2) * 128 + 128],
                            ARB[:, EVA + k * 128:EVA + (k + 1) * 128], identb, start=True, stop=True))
                    w("vector", c_t)
                    for k in range(half * 4, half * 4 + 4):
                        vec(lambda q, k=k: q.tensor_copy(
                            ARB[:, EVB + k * 128:EVB + (k + 1) * 128],
                            pbank[4 + (k % 4) % 2][:, ((k % 4) // 2) * 128:((k % 4) // 2) * 128 + 128]))
                w("gpsimd", c_v)
                for k in range(KC):
                    dma_g(s_qf[k, :, tt * 128:(tt + 1) * 128], ARB[:, EVB + k * 128:EVB + (k + 1) * 128])

            if phase_limit < 4:
                raise _PhaseStop()
            wstg = ARF[:, 0:4096]
            ZOFF = 40960
            w("sync", c_gdma)
            for k in range(KC):
                dma_g(ARB[:, k * 4096:(k + 1) * 4096], d_w_inT[k * 128:(k + 1) * 128, :])
            t_wcast = c_gdma.val
            if phase_limit < 4.2:
                raise _PhaseStop()
            _lhs = [0] if phase_limit < 4.6 else range(2)
            _ms = [0] if phase_limit < 4.3 else (range(GC) if phase_limit < 4.4 else range(2 * E // 128))
            for lh in _lhs:
                lo = lh * 1024
                w("sync", c_t)
                if phase_limit < 4.22:
                    raise _PhaseStop()
                t_un = 0
                for k in range(KC if phase_limit >= 4.24 else 1):
                    t_un = dma_in(ARB[:, 32768 + k * 1024:32768 + (k + 1) * 1024], s_un[k, :, lo:lo + 1024])
                if phase_limit < 4.25:
                    raise _PhaseStop()
                for m in _ms:
                    for lc in range(2):
                        pb = pbank[(m * 2 + lc) % 4]
                        w("tensor", c_dma, t_un)
                        w("tensor", c_gdma, t_wcast)
                        w("tensor", c_v)
                        w("tensor", c_s)
                        for k in range(KC - 1):
                            ten0(lambda q, k=k, m=m, lc=lc, pb=pb: q.matmul(
                                pb[:, 0:512],
                                ARB[:, k * 4096 + m * 128:k * 4096 + (m + 1) * 128],
                                ARB[:, 32768 + k * 1024 + lc * 512:32768 + k * 1024 + (lc + 1) * 512],
                                start=(k == 0), stop=False))
                        t_mm = ten(lambda q, k=KC - 1, m=m, lc=lc, pb=pb: q.matmul(
                            pb[:, 0:512],
                            ARB[:, k * 4096 + m * 128:k * 4096 + (m + 1) * 128],
                            ARB[:, 32768 + k * 1024 + lc * 512:32768 + k * 1024 + (lc + 1) * 512],
                            start=False, stop=True))
                        if phase_limit < 4.27:
                            continue
                        if m < GC:
                            w("vector", c_t, t_mm)
                            w("vector", c_gdma)
                            t_e = vec(lambda q, pb=pb, lc=lc: q.tensor_copy(
                                ARB[:, ZOFF + lc * 512:ZOFF + (lc + 1) * 512], pb[:, 0:512]))
                            if phase_limit < 4.28:
                                continue
                            w("gpsimd", c_v, t_e)
                            dma_g(s_xr[m, :, lo + lc * 512:lo + (lc + 1) * 512],
                                  ARB[:, ZOFF + lc * 512:ZOFF + (lc + 1) * 512])
                        else:
                            g = m - GC
                            w("scalar", c_t, t_mm)
                            w("scalar", c_gdma)
                            t_e = sca(lambda q, pb=pb, lc=lc: q.activation(
                                out=ARB[:, ZOFF + 1024 + lc * 512:ZOFF + 1024 + (lc + 1) * 512],
                                in_=pb[:, 0:512], func=AF.Silu))
                            w("gpsimd", c_s, t_e)
                            dma_g(s_z[g, :, lo + lc * 512:lo + (lc + 1) * 512],
                                  ARB[:, ZOFF + 1024 + lc * 512:ZOFF + 1024 + (lc + 1) * 512])
            t_phB_g = c_gdma.val

            if phase_limit < 5:
                raise _PhaseStop()
            XIN = 8192
            w("gpsimd", c_t)
            dma_g(ARB[:, 0:4096], d_convd[:, 0:4096])
            dma_g(ARB[:, 4096:8192], d_convd[:, 4096:8192])
            dma_g(ARB[:, 10752:12288], d_w_xT)
            dma_g(ARB[:, XIN:XIN + 4], d_zero4)
            t_wc = c_gdma.val
            for g in range(GC):
                w("sync", c_t)
                t_l = dma_in(ARB[:, XIN + 4:XIN + 4 + L], s_xr[g, :, :])
                w("tensor", c_dma, t_l)
                w("tensor", c_gdma, t_wc)
                w("tensor", c_s)
                w("tensor", c_v)
                for lc in range(4):
                    pb = pbank[lc % 2]
                    if lc >= 2:
                        w("tensor", c_s)
                    for tap in range(3):
                        off = (tap * GC + g) * 128
                        ten0(lambda q, off=off, lc=lc, tap=tap, pb=pb: q.matmul(
                            pb[:, 0:512],
                            ARB[:, off:off + 128],
                            ARB[:, XIN + 1 + lc * 512 + tap:XIN + 1 + lc * 512 + tap + 512],
                            start=(tap == 0), stop=False))
                    _off3 = (3 * GC + g) * 128
                    t_mm = ten(lambda q, off=_off3, lc=lc, pb=pb: q.matmul(
                        pb[:, 0:512],
                        ARB[:, off:off + 128],
                        ARB[:, XIN + 1 + lc * 512 + 3:XIN + 1 + lc * 512 + 3 + 512],
                        start=False, stop=True))
                    w("scalar", c_t, t_mm)
                    w("scalar", c_gdma)
                    t_e = sca(lambda q, pb=pb, lc=lc, g=g: q.activation(
                        out=ARB[:, ZOFF + lc * 512:ZOFF + (lc + 1) * 512], in_=pb[:, 0:512],
                        func=AF.Silu, bias=v_convb[:, g:g + 1]))
                    w("gpsimd", c_s, t_e)
                    dma_g(s_xc[g, :, lc * 512:(lc + 1) * 512], ARB[:, ZOFF + lc * 512:ZOFF + (lc + 1) * 512])

            if phase_limit < 6:
                raise _PhaseStop()
            XCT = 0
            w("sync", c_gdma)
            DTR, BRO, DTR2 = 24576, 26624, 28672
            WDT = 16384
            t_wdt = dma_g(ARB[0:R, WDT:WDT + E], d_w_dtT)
            for lc in range(4):
                w("sync", c_t)
                t_x = 0
                for g in range(GC):
                    t_x = dma_in(ARB[:, XCT + g * 512:XCT + (g + 1) * 512], s_xc[g, :, lc * 512:(lc + 1) * 512])
                pb = pbank[4 + lc % 2]
                w("tensor", c_dma, t_x)
                w("tensor", c_v)
                for g in range(GC - 1):
                    ten0(lambda q, g=g, pb=pb: q.matmul(
                        pb[0:96, 0:512],
                        ARB[:, 10752 + g * 96:10752 + (g + 1) * 96],
                        ARB[:, XCT + g * 512:XCT + (g + 1) * 512],
                        start=(g == 0), stop=False))
                t_mm = ten(lambda q, g=GC - 1, pb=pb: q.matmul(
                    pb[0:96, 0:512],
                    ARB[:, 10752 + g * 96:10752 + (g + 1) * 96],
                    ARB[:, XCT + g * 512:XCT + (g + 1) * 512],
                    start=False, stop=True))
                w("vector", c_t, t_mm)
                vec(lambda q, pb=pb, lc=lc: q.tensor_copy(ARB[0:96, DTR + lc * 512:DTR + (lc + 1) * 512],
                                                          pb[0:96, 0:512]))
            w("gpsimd", c_v)
            t_rows = dma_g(ARB[0:32, BRO:BRO + L], ARB[64:96, DTR:DTR + L])
            t_rows = dma_g(ARB[0:64, DTR2:DTR2 + L], ARB[0:64, DTR:DTR + L])
            if True:
                pass

            if phase_limit < 7:
                raise _PhaseStop()
            if debug:
                w("gpsimd", c_v)
                dma_g(s_dbg2[:, 0:L], ARB[:, DTR:DTR + L])
                w("gpsimd", c_gdma)
                dma_g(s_dbg2[0:32, L:2 * L], ARB[0:32, BRO:BRO + L])
            dtg = ARF[:, 4096:6144]
            dtg2 = ARF[:, 6144:8192]
            w("tensor", c_gdma)
            for g in range(GC):
                w("tensor", c_v)
                for lc in range(4):
                    ten(lambda q, g=g, lc=lc: q.matmul(
                        pbank[lc % 4][:, 0:512],
                        ARB[0:R, WDT + g * 128:WDT + (g + 1) * 128],
                        ARB[0:R, DTR2 + lc * 512:DTR2 + (lc + 1) * 512],
                        start=True, stop=True))
                t_mm = c_t.val
                w("vector", c_t, t_mm)
                w("vector", c_gdma)
                w("vector", c_s)
                for lc in range(4):
                    vec(lambda q, g=g, lc=lc: q.tensor_scalar(
                        dtg[:, lc * 512:(lc + 1) * 512], pbank[lc % 4][:, 0:512],
                        v_bdt[:, g:g + 1], None, ALU.add))
                if debug and g == 0:
                    w("gpsimd", c_v)
                    dma_g(s_dbg3[:, 0:L], dtg)
                w("scalar", c_v)
                sca(lambda q: q.activation(out=dtg2, in_=dtg, func=AF.Abs))
                sca(lambda q: q.activation(out=dtg2, in_=dtg2, func=AF.Exp, scale=-1.0))
                sca(lambda q: q.activation(out=dtg2, in_=dtg2, func=AF.Ln, bias=1.0))
                w("vector", c_s)
                vec(lambda q: q.scalar_tensor_tensor(dtg, dtg, 0.0, dtg2, ALU.max, ALU.add))
                w("gpsimd", c_v)
                dma_g(s_dt[g, :, :], dtg)

            if phase_limit < 8:
                raise _PhaseStop()
            XTG, U2G, DBU, SNG, SCI = 0, 2048, 4096, 6144, 8192
            YTM = DBU
            BCB, BCC = 30720, 47104
            dAg2 = [ARF[:, 8192:10240], ARF[:, 16384:18432]]
            yg = ARF[:, 10240:12288]
            yg2 = ARF[:, 12288:14336]
            tr1 = [ARF[:, 0:2048], ARF[:, 2048:4096], ARF[:, 6144:8192], yg]
            dbu2 = [ARB[:, DBU:DBU + L], ARF[:, 14336:15360].bitcast(BF16)]
            dtg_nxa = ARB[:, 24576:26624].bitcast(F32)
            dtg_nxb = ARB[:, 28672:30720].bitcast(F32)
            xtg_nx = ARF[:, 15360:16384].bitcast(BF16)
            w("sync", c_t)
            w("sync", c_gdma)
            for oct_ in range(2):
                if oct_ == 0:
                    w("tensor", c_gdma, t_rows)
                for i in range(NOCT):
                    n = oct_ * NOCT + i
                    for (srow, dstoff) in ((n, BCB), (16 + n, BCC)):
                        for lc in range(4):
                            pb = pbank[4 + (i * 8 + lc) % 2]
                            w("tensor", c_v)
                            ten(lambda q, srow=srow, lc=lc, pb=pb: q.matmul(
                                pb[:, 0:512], sel32[:, srow * 128:(srow + 1) * 128],
                                ARB[0:32, BRO + lc * 512:BRO + (lc + 1) * 512],
                                start=True, stop=True))
                            w("vector", c_t)
                            vec(lambda q, i=i, dstoff=dstoff, lc=lc, pb=pb: q.tensor_copy(
                                ARB[:, dstoff + i * 2048 + lc * 512:dstoff + i * 2048 + (lc + 1) * 512],
                                pb[:, 0:512]))
                t_bc = c_v.val

                if oct_ == 1:
                    w("sync", c_gdma)
                t_pref = {}
                for g in range(GC):
                    if g == 0:
                        w("sync", c_v)
                        t_x = dma_in(ARB[:, XTG:XTG + L], s_xc[g, :, :])
                        w("sync", c_s)
                        t_dtl = dma_in(dtg, s_dt[g, :, :])
                    else:
                        t_x, t_dtl = t_pref[g]
                        w("vector", c_dma, t_dtl)
                        w("vector", c_s)
                        vec(lambda q: q.tensor_copy(ARB[:, XTG:XTG + L], xtg_nx))
                        vec(lambda q: q.tensor_copy(dtg[:, 0:1024], dtg_nxa))
                        vec(lambda q: q.tensor_copy(dtg[:, 1024:2048], dtg_nxb))
                        t_dtl = 0
                    w("vector", c_dma, max(t_x, t_dtl))
                    vec(lambda q: q.tensor_tensor(ARB[:, U2G:U2G + L], dtg, ARB[:, XTG:XTG + L], ALU.mult))
                    t_u2 = c_v.val
                    w("scalar", c_v, t_u2)
                    if g + 1 < GC:
                        w("sync", c_v, t_u2)
                        t_xn_ = dma_in(xtg_nx, s_xc[g + 1, :, :])
                        dma_in(dtg_nxa, s_dt[g + 1, :, 0:1024])
                        t_dn_ = dma_in(dtg_nxb, s_dt[g + 1, :, 1024:2048])
                        t_pref[g + 1] = (t_xn_, t_dn_)
                    w("gpsimd", c_v, t_u2)
                    t_tts = {}
                    t_dbu = {}
                    for i in range(NOCT):
                        n = oct_ * NOCT + i
                        dAg = dAg2[i % 2]
                        dbu = dbu2[i % 2]
                        if i >= 2:
                            w("scalar", c_v, t_tts[i - 2])
                        t_dA = sca(lambda q, g=g, n=n, dAg=dAg: q.activation(
                            out=dAg, in_=dtg, func=AF.Exp, scale=v_negA[:, g * N + n:g * N + n + 1]))
                        vec(lambda q, i=i, dbu=dbu: q.tensor_tensor(
                            dbu, ARB[:, U2G:U2G + L],
                            ARB[:, BCB + i * 2048:BCB + (i + 1) * 2048], ALU.mult))
                        w("vector", c_s, t_dA)
                        t_tts[i] = vec(lambda q, dAg=dAg, dbu=dbu: q.tensor_tensor_scan(
                            ARB[:, SNG:SNG + L], dAg, dbu, 0.0, ALU.mult, ALU.add))
                        vec(lambda q, i=i: q.tensor_tensor(
                            ARB[:, SCI + i * 2048:SCI + (i + 1) * 2048],
                            ARB[:, SNG:SNG + L],
                            ARB[:, BCC + i * 2048:BCC + (i + 1) * 2048], ALU.mult))
                    t_sc_all = c_v.val
                    w("vector", c_gdma)
                    vec(lambda q: q.tensor_tensor(
                        tr1[0], ARB[:, SCI:SCI + 2048], ARB[:, SCI + 2048:SCI + 4096], ALU.add))
                    vec(lambda q: q.tensor_tensor(
                        tr1[1], ARB[:, SCI + 4096:SCI + 6144], ARB[:, SCI + 6144:SCI + 8192], ALU.add))
                    vec(lambda q: q.tensor_tensor(
                        tr1[2], ARB[:, SCI + 8192:SCI + 10240], ARB[:, SCI + 10240:SCI + 12288], ALU.add))
                    vec(lambda q: q.tensor_tensor(
                        tr1[3], ARB[:, SCI + 12288:SCI + 14336], ARB[:, SCI + 14336:SCI + 16384], ALU.add))
                    vec(lambda q: q.tensor_tensor(tr1[0], tr1[0], tr1[1], ALU.add))
                    vec(lambda q: q.tensor_tensor(tr1[2], tr1[2], tr1[3], ALU.add))
                    t_yo = vec(lambda q: q.tensor_tensor(yg2, tr1[0], tr1[2], ALU.add))
                    if oct_ == 0:
                        w("gpsimd", c_v, t_yo)
                        dma_g(s_y0[g, :, :], yg2)
                    else:
                        w("sync", c_v)
                        t_y0 = dma_in(yg, s_y0[g, :, :])
                        t_z = dma_in(ARB[:, YTM:YTM + L], s_z[g, :, :])
                        w("vector", c_dma, t_z)
                        vec(lambda q: q.tensor_tensor(yg, yg, yg2, ALU.add))
                        vec(lambda q, g=g: q.scalar_tensor_tensor(
                            yg, ARB[:, XTG:XTG + L], v_dpar[:, g:g + 1], yg, ALU.mult, ALU.add))
                        w("vector", c_gdma)
                        t_y2 = vec(lambda q: q.tensor_tensor(
                            ARB[:, SNG:SNG + L], yg, ARB[:, YTM:YTM + L], ALU.mult))
                        w("gpsimd", c_v, t_y2)
                        dma_g(s_y2[g, :, :], ARB[:, SNG:SNG + L])
            t_phE_g = c_gdma.val

            if phase_limit < 9:
                raise _PhaseStop()
            Y2T, QFT = 30720, 38912
            outf = ARF[:, 4096:12288]
            evacf = ARF[:, 17408:17920]
            xnH = ARF[:, 14336:15360]
            uH = ARF[:, 15360:16384]
            oH = ARF[:, 16384:17408]
            w("gpsimd", c_v)
            for g in range(GC):
                dma_g(ARB[:, g * 1024:(g + 1) * 1024], d_w_outT[g * 128:(g + 1) * 128, :])
            for k in range(KC):
                dma_g(ARB[:, 16384 + k * 1024:16384 + (k + 1) * 1024], d_ffn_wT[k * 128:(k + 1) * 128, :])
            t_phF_w = c_gdma.val

            for lh in range(2):
                lo = lh * 1024
                for lc in range(2):
                    l0 = lo + lc * 512
                    w("sync", c_t)
                    w("sync", c_gdma, t_phF_w)
                    t_y2 = 0
                    for g in range(GC):
                        t_y2 = dma_in(ARB[:, Y2T + g * 512:Y2T + (g + 1) * 512], s_y2[g, :, l0:l0 + 512])
                    w("tensor", c_dma, t_y2)
                    w("tensor", c_gdma, t_phF_w)
                    for m in range(KC):
                        pb = pbank[m % 4]
                        w("tensor", c_v)
                        for g in range(GC - 1):
                            ten0(lambda q, g=g, m=m, pb=pb: q.matmul(
                                pb[:, 0:512],
                                ARB[:, g * 1024 + m * 128:g * 1024 + (m + 1) * 128],
                                ARB[:, Y2T + g * 512:Y2T + (g + 1) * 512],
                                start=(g == 0), stop=False))
                        t_mm = ten(lambda q, g=GC - 1, m=m, pb=pb: q.matmul(
                            pb[:, 0:512],
                            ARB[:, g * 1024 + m * 128:g * 1024 + (m + 1) * 128],
                            ARB[:, Y2T + g * 512:Y2T + (g + 1) * 512],
                            start=False, stop=True))
                        w("vector", c_t, t_mm)
                        vec(lambda q, m=m, lc=lc, pb=pb: q.tensor_copy(
                            outf[:, m * 1024 + lc * 512:m * 1024 + (lc + 1) * 512], pb[:, 0:512]))
                    w("sync", c_t)
                    t_qf = 0
                    for k in range(KC):
                        t_qf = dma_in(ARB[:, QFT + k * 512:QFT + (k + 1) * 512], s_qf[k, :, l0:l0 + 512])
                    w("tensor", c_dma, t_qf)
                    for m in range(KC):
                        pb = pbank[4 + m % 2]
                        w("tensor", c_s)
                        w("tensor", c_v)
                        for k in range(KC - 1):
                            ten0(lambda q, k=k, m=m, pb=pb: q.matmul(
                                pb[:, 0:512],
                                ARB[:, 16384 + k * 1024 + m * 128:16384 + k * 1024 + (m + 1) * 128],
                                ARB[:, QFT + k * 512:QFT + (k + 1) * 512],
                                start=(k == 0), stop=False))
                        t_mm = ten(lambda q, k=KC - 1, m=m, pb=pb: q.matmul(
                            pb[:, 0:512],
                            ARB[:, 16384 + k * 1024 + m * 128:16384 + k * 1024 + (m + 1) * 128],
                            ARB[:, QFT + k * 512:QFT + (k + 1) * 512],
                            start=False, stop=True))
                        w("scalar", c_t, t_mm)
                        w("scalar", c_v)
                        t_ffn = sca(lambda q, m=m, pb=pb: q.activation(
                            out=evacf[:, 0:512], in_=pb[:, 0:512],
                            func=AF.Identity, bias=v_fbias[:, m:m + 1], scale=fsc2[:, m:m + 1]))
                        w("vector", c_s, t_ffn)
                        vec(lambda q, m=m, lc=lc: q.tensor_tensor(
                            outf[:, m * 1024 + lc * 512:m * 1024 + (lc + 1) * 512],
                            outf[:, m * 1024 + lc * 512:m * 1024 + (lc + 1) * 512],
                            evacf[:, 0:512], ALU.add))

                for ti in range(8):
                    tt = lh * 8 + ti
                    w("sync", c_v)
                    t_u = dma_in(uH, d_u[tt * 128:(tt + 1) * 128, :])
                    t_x = dma_in(xnH, s_xn[tt * 128:(tt + 1) * 128, :])
                    w("tensor", c_v)
                    for k in range(KC):
                        ten(lambda q, k=k, ti=ti: q.transpose(
                            pbank[k % 4][:, (k // 4) * 128:(k // 4) * 128 + 128],
                            outf[:, k * 1024 + ti * 128:k * 1024 + ti * 128 + 128], identf))
                    t_tr = c_t.val
                    w("vector", c_t, t_tr)
                    w("vector", c_dma, t_x)
                    w("vector", c_gdma)
                    for k in range(KC):
                        vec(lambda q, k=k: q.tensor_tensor(
                            oH[:, k * 128:(k + 1) * 128],
                            pbank[k % 4][:, (k // 4) * 128:(k // 4) * 128 + 128],
                            uH[:, k * 128:(k + 1) * 128], ALU.add))
                    t_o = vec(lambda q: q.tensor_tensor(oH, oH, xnH, ALU.add))
                    w("gpsimd", c_v, t_o)
                    dma_g(d_out[tt * 128:(tt + 1) * 128, :], oH)


        except _PhaseStop:
            pass

        emit("gpsimd", lambda q, t=c_gdma.val: q.wait_ge(sm_gdma, t))

        @block.sync
        def _(q):
            for fn in streams["sync"]:
                fn(q)

        @block.scalar
        def _(q):
            for fn in streams["scalar"]:
                fn(q)

        @block.vector
        def _(q):
            for fn in streams["vector"]:
                fn(q)

        @block.tensor
        def _(q):
            for fn in streams["tensor"]:
                fn(q)

        @block.gpsimd
        def _(q):
            for fn in streams["gpsimd"]:
                fn(q)

    return nc


_NC_CACHE = {}


def _get_nc(debug=False):
    if debug not in _NC_CACHE:
        _NC_CACHE[debug] = build_program(debug)
    return _NC_CACHE[debug]


def _sel32():
    sel = np.zeros((32, 32 * 128), dtype=ml_dtypes.bfloat16)
    for t in range(32):
        sel[t, t * 128:(t + 1) * 128] = 1
    return sel


def make_inputs(L0_attn_u, L0_attn_w_norm, L0_attn_w_in, L0_attn_conv_w, L0_attn_conv_b,
                L0_attn_w_x, L0_attn_w_dt, L0_attn_b_dt, L0_attn_A_log, L0_attn_D_param,
                L0_attn_w_out, L0_ffn_weight_int8, L0_ffn_weight_scale, L0_ffn_bias):
    u = np.ascontiguousarray(np.asarray(L0_attn_u, dtype=np.float32))
    w_inT = np.ascontiguousarray(np.asarray(L0_attn_w_in, dtype=np.float32).T)
    w_xT = np.ascontiguousarray(np.asarray(L0_attn_w_x, dtype=np.float32).T.reshape(GC, 128, 96).transpose(1, 0, 2).reshape(128, GC * 96))
    w_dtT = np.ascontiguousarray(np.asarray(L0_attn_w_dt, dtype=np.float32).T)
    w_outT = np.ascontiguousarray(np.asarray(L0_attn_w_out, dtype=np.float32).T)
    conv_w = np.asarray(L0_attn_conv_w, dtype=np.float32)
    ffn_wT = np.ascontiguousarray(np.asarray(L0_ffn_weight_int8).T)

    convd = np.zeros((4 * GC, 128, 128), np.float32)
    idx = np.arange(128)
    for tap in range(4):
        for g in range(GC):
            convd[tap * GC + g, idx, idx] = conv_w[g * 128:(g + 1) * 128, 0, tap]
    convd = np.ascontiguousarray(convd.transpose(1, 0, 2).reshape(128, 4 * GC * 128))

    def cols(v, k):
        return np.ascontiguousarray(np.asarray(v, dtype=np.float32).reshape(k, 128).T)

    alog = np.asarray(L0_attn_A_log, dtype=np.float32).reshape(GC, 128, N)
    shared = {
        "w_inT": w_inT, "w_xT": w_xT, "w_dtT": w_dtT, "w_outT": w_outT,
        "convd": convd, "ffn_wT": ffn_wT,
        "wnorm_c": cols(L0_attn_w_norm, KC),
        "convb_c": cols(L0_attn_conv_b, GC),
        "bdt_c": cols(L0_attn_b_dt, GC),
        "dpar_c": cols(L0_attn_D_param, GC),
        "alog_c": np.ascontiguousarray(alog.transpose(1, 0, 2).reshape(128, GC * N)),
        "fscale_c": cols(np.asarray(L0_ffn_weight_scale, np.float32).reshape(-1), KC),
        "fbias_c": cols(L0_ffn_bias, KC),
        "identf": np.eye(128, dtype=np.float32),
        "identb_in": np.eye(128, dtype=ml_dtypes.bfloat16),
        "zero4": np.zeros((128, 4), dtype=ml_dtypes.bfloat16),
        "sel32": _sel32(),
    }
    return [dict(shared, u=np.ascontiguousarray(u[b])) for b in range(8)]


def kernel(**inputs):
    in_maps = make_inputs(**inputs)
    nc = _get_nc(False)
    res = run_bass_kernel_spmd(nc, in_maps, core_ids=list(range(8)))
    out = np.stack([res.results[b]["out"] for b in range(8)], axis=0)
    return out.astype(np.float32)
